# version 09 (final)
# speedup vs baseline: 1.0978x; 1.0031x over previous
import numpy as np
import ml_dtypes
import concourse.bass as bass
import concourse.mybir as mybir
from concourse.bass_utils import run_bass_kernel_spmd
from concourse.expressions_rust import RuntimeValue

F32 = mybir.dt.float32
BF16 = mybir.dt.bfloat16
I32 = mybir.dt.int32
AF = mybir.ActivationFunctionType
ALU = mybir.AluOpType

NCORE = 8
B, T, D = 2, 8192, 1024
NMB, MB = 8, 1024
MLA_H, MLA_HD, MLA_ROPE, LAT = 16, 64, 32, 128
FFN_L, FFN_H, FFN_DH, WIN = 2, 8, 64, 512
HID, HIDP, INNER = 2730, 2816, 512
NEG = -3.0e38

import os as _os
DEBUG_STAGE = int(_os.environ.get("K_DBGSTAGE", "3"))
KVH_W = 4 * 512 + 4 * 520
ENGS = ("sync", "tensor", "vector", "scalar", "gpsimd")


def ap(t, prow, pstep, pcnt, col, clist):
    W = t.shape[1]
    return bass.AP(t, prow * W + col, [[pstep * W, pcnt]] + clist)



def _mla_perm():
    rope_cols, nr_cols = [], []
    for h in range(MLA_H):
        rope_cols.extend(range(MLA_HD * h, MLA_HD * h + MLA_ROPE))
        nr_cols.extend(range(MLA_HD * h + MLA_ROPE, MLA_HD * (h + 1)))
    return np.array(rope_cols + nr_cols, dtype=np.int64)


def _rope_tables(tok_pos):
    half_idx = np.arange(0, FFN_DH, 2, dtype=np.float64)
    inv_freq = 1.0 / 10000.0 ** (half_idx / FFN_DH)
    t = np.asarray(tok_pos, dtype=np.float64)
    freqs = t[:, None] * inv_freq[None, :]
    freqs = np.concatenate([freqs, freqs], axis=-1)
    base = (half_idx + 0.4 * FFN_DH) / (1.4 * FFN_DH)
    power = (t - T // 2) / float(WIN // 2)
    scale = base[None, :] ** power[:, None]
    scale = np.concatenate([scale, scale], axis=-1)
    cos, sin = np.cos(freqs), np.sin(freqs)
    tabs = []
    for arr in (cos * scale, sin * scale, cos / scale, sin / scale):
        a = arr.T.astype(np.float32)
        tabs.append(np.concatenate([a, a], axis=0))
    return tabs


def shared_inputs(inputs):
    d = {}
    d["x0f"] = np.ascontiguousarray(inputs["L0_attn_x"].reshape(B * T, D))
    perm = _mla_perm()
    d["wq"] = np.ascontiguousarray(inputs["L0_attn_Wq"][:, perm])
    d["wkv"] = np.ascontiguousarray(inputs["L0_attn_Wkv"])
    d["wk"] = np.ascontiguousarray(inputs["L0_attn_Wk"])
    d["wv"] = np.ascontiguousarray(inputs["L0_attn_Wv"])
    wo = inputs["L0_attn_Wo"]
    wop = np.zeros((1024, 1024), dtype=np.float32)
    for pp in range(8):
        wop[128 * pp + 0:128 * pp + 32] = wo[64 * 2 * pp: 64 * 2 * pp + 32]
        wop[128 * pp + 64:128 * pp + 96] = wo[64 * (2 * pp + 1): 64 * (2 * pp + 1) + 32]
    d["wo"] = wop
    d["qg_rope"] = np.tile(inputs["L0_attn_q_g"][:32], MLA_H).astype(np.float32).reshape(512, 1)
    d["kg"] = np.tile(inputs["L0_attn_k_g"], MLA_H).astype(np.float32).reshape(512, 1)

    d["ident_bf"] = np.eye(128, dtype=ml_dtypes.bfloat16)
    d["ident_f"] = np.eye(128, dtype=np.float32)
    tri = np.zeros((128, 128), dtype=np.float32)
    for ss in range(128):
        tri[ss, :ss] = NEG
    d["tri"] = tri
    ind4 = np.zeros((128, 128), dtype=np.float32)
    for r in range(128):
        ind4[r, r // 32] = 1.0
    d["ind4"] = ind4.astype(ml_dtypes.bfloat16)
    bc4 = np.zeros((128, 128), dtype=ml_dtypes.bfloat16)
    for r in range(128):
        bc4[r // 32, r] = 1.0
    d["bc4"] = bc4
    bc2 = np.zeros((128, 128), dtype=ml_dtypes.bfloat16)
    bc2[32, 0:32] = 1.0
    bc2[96, 64:96] = 1.0
    d["bc2"] = bc2
    statx = np.zeros((128, 128), dtype=ml_dtypes.bfloat16)
    statx[:, 0] = 1.0
    d["statx"] = statx
    onesrow = np.zeros((128, 128), dtype=ml_dtypes.bfloat16)
    onesrow[0, :] = 1.0
    onesrow[64, :] = 1.0
    d["onesrow"] = onesrow
    ind2 = np.zeros((128, 128), dtype=np.float32)
    for r in range(128):
        ind2[r, r // 64] = 1.0
    d["ind2"] = ind2.astype(ml_dtypes.bfloat16)
    bch2 = np.zeros((128, 128), dtype=ml_dtypes.bfloat16)
    for r in range(128):
        bch2[r // 64, r] = 1.0
    d["bch2"] = bch2
    mleft = np.zeros((128, 128), dtype=np.float32)
    for ss in range(128):
        mleft[ss, :ss] = NEG
    d["mleft"] = mleft
    mright = np.zeros((128, 128), dtype=np.float32)
    for ss in range(128):
        mright[ss, ss + 1:] = NEG
    d["mright"] = mright
    rt = np.zeros((128, 128), dtype=np.float32)
    for blk in range(2):
        b0 = 64 * blk
        for m in range(32):
            rt[b0 + m + 32, b0 + m] = -1.0
            rt[b0 + m, b0 + m + 32] = 1.0
    d["rotm"] = rt.astype(ml_dtypes.bfloat16)

    wqkv = inputs["L0_ffn_wqkv"]
    d["fwqkv"] = np.ascontiguousarray(
        wqkv.reshape(2, 8, 128, 12, 128).transpose(0, 3, 1, 2, 4))
    wo_f = inputs["L0_ffn_wo"]
    wof = np.zeros((FFN_L, 8, 128, 1024), dtype=np.float32)
    for h in range(FFN_H):
        wof[:, h, 0:64, :] = wo_f[:, 64 * h:64 * h + 64, :]
    d["fwo"] = np.ascontiguousarray(
        wof.reshape(2, 8, 128, 8, 128).transpose(0, 1, 3, 2, 4))
    w1 = inputs["L0_ffn_w1"]
    w1a = np.zeros((FFN_L, 1024, HIDP), dtype=np.float32)
    w1g = np.zeros((FFN_L, 1024, HIDP), dtype=np.float32)
    w1a[:, :, :HID] = w1[:, :, :HID]
    w1g[:, :, :HID] = w1[:, :, HID:]
    d["fw1a"] = np.ascontiguousarray(
        w1a.reshape(2, 8, 128, 22, 128).transpose(0, 3, 1, 2, 4))
    d["fw1g"] = np.ascontiguousarray(
        w1g.reshape(2, 8, 128, 22, 128).transpose(0, 3, 1, 2, 4))
    w2 = inputs["L0_ffn_w2"]
    w2p = np.zeros((FFN_L, HIDP, 1024), dtype=np.float32)
    w2p[:, :HID] = w2
    d["fw2"] = np.ascontiguousarray(
        w2p.reshape(2, 22, 128, 8, 128).transpose(0, 3, 1, 2, 4))
    b1 = inputs["L0_ffn_b1"]
    b1a = np.zeros((FFN_L, HIDP), dtype=np.float32)
    b1g = np.zeros((FFN_L, HIDP), dtype=np.float32)
    b1a[:, :HID] = b1[:, :HID]
    b1g[:, :HID] = b1[:, HID:]
    d["fb1a"] = b1a.reshape(FFN_L, 22, 128).transpose(0, 2, 1).copy()
    d["fb1g"] = b1g.reshape(FFN_L, 22, 128).transpose(0, 2, 1).copy()
    d["fb2"] = inputs["L0_ffn_b2"].reshape(FFN_L, 8, 128).transpose(0, 2, 1).copy()
    d["flng"] = inputs["L0_ffn_ln_g"].reshape(FFN_L, 8, 128).transpose(0, 2, 1).copy()
    d["flnb"] = inputs["L0_ffn_ln_b"].reshape(FFN_L, 8, 128).transpose(0, 2, 1).copy()
    d["fqs"] = np.concatenate([inputs["L0_ffn_q_scale"]] * 2, axis=1).reshape(FFN_L, 128, 1).astype(np.float32)
    d["fks"] = np.concatenate([inputs["L0_ffn_k_scale"]] * 2, axis=1).reshape(FFN_L, 128, 1).astype(np.float32)
    return d


def core_inputs(core):
    pa = core + 1
    pb = NMB - core
    tokA0 = (pa - 1) * MB
    tokB0 = (pb - 1) * MB
    it = []
    for j in range(9):
        if j == 0:
            ch, mb = 0, pa - 1
        elif j < pa:
            ch, mb = 0, j - 1
        elif j < 8:
            ch, mb = 1, j - pa
        else:
            ch, mb = 1, pb - 1
        it.append((ch, mb))
    params = np.zeros((1, 128), dtype=np.int32)
    for j, (ch, mb) in enumerate(it):
        params[0, 4 * j + 0] = (ch * T + mb * MB) * D
        params[0, 4 * j + 1] = ch * (512 * MB)
        params[0, 4 * j + 2] = ch * (1024 * MB)
    params[0, 40] = (0 * T + tokA0) * D
    params[0, 41] = (1 * T + tokB0) * D
    for ch, t0 in ((0, tokA0), (1, T + tokB0)):
        for t8 in range(8):
            params[0, 54 + ch * 8 + t8] = (t0 + t8 * 128) * D
    slotA = (core - 1) * 2 + 0 if core >= 1 else 16
    slotB = (core + 1) * 2 + 1 if core + 1 < NCORE else 16
    params[0, 50] = slotA * 128 * KVH_W
    params[0, 51] = slotB * 128 * KVH_W
    params[0, 48] = slotA * 128 * KVH_W + 2048
    params[0, 49] = slotB * 128 * KVH_W + 2048
    params[0, 52] = slotA * 128 * 8
    params[0, 53] = slotB * 128 * 8
    d = {"params": params}
    pos = np.concatenate([np.arange(tokA0, tokA0 + MB), np.arange(tokB0, tokB0 + MB)])
    cxs, sxs, cds, sds = _rope_tables(pos)
    d["rope_cxs"], d["rope_sxs"], d["rope_cds"], d["rope_sds"] = cxs, sxs, cds, sds
    hk = np.zeros((2, 128, 1), dtype=np.float32)
    if pa == 1:
        hk[0] = NEG
    if pb == 1:
        hk[1] = NEG
    d["halo_kill"] = hk
    return d, (pa, pb)



class Sched:
    def __init__(self, nc):
        self.nc = nc
        self.sem = {}
        self.cnt = {}
        self.q = {e: [] for e in ENGS}

    def _app(self, eng, f):
        import traceback
        fr = traceback.extract_stack()[-3]
        f._src = f"{fr.filename}:{fr.lineno}"
        self.q[eng].append(f)

    def new_sems(self, *names):
        for n in names:
            self.sem[n] = self.nc.alloc_semaphore(n)
            self.cnt[n] = 0

    def dma(self, eng, out, in_, sname, **kw):
        self.cnt[sname] += 16
        h = self.sem[sname]

        def f(e, out=out, in_=in_, kw=kw, h=h):
            e.dma_start(out=out, in_=in_, **kw).then_inc(h, 16)
        self._app(eng, f)

    def op(self, eng, fn, sname=None, n=1):
        if sname is None:
            self._app(eng, fn)
        else:
            self.cnt[sname] += n
            h = self.sem[sname]

            def f(e, fn=fn, h=h, n=n):
                fn(e).then_inc(h, n)
            self._app(eng, f)

    def dyn_dma(self, eng, reg, lo, hi, out_fn, in_fn, sname, **kw):
        self.cnt[sname] += 16
        h = self.sem[sname]

        def f(e, reg=reg, lo=lo, hi=hi, out_fn=out_fn, in_fn=in_fn, kw=kw, h=h):
            sv = e.snap(reg, min_val=lo, max_val=hi)
            e.dma_start(out=out_fn(sv), in_=in_fn(sv), **kw).then_inc(h, 16)
            e.free_register(sv.val)
        self._app(eng, f)

    _dynregs = {}

    def pdyn(self, eng, psb, pcol, lo, hi, out_fn, in_fn, sname, **kw):
        self.cnt[sname] += 16
        h = self.sem[sname]
        self.cnt["sRG"] += 1
        hr = self.sem["sRG"]
        nrg = self.cnt["sRG"]

        def f(e, eng=eng, psb=psb, pcol=pcol, lo=lo, hi=hi, out_fn=out_fn,
              in_fn=in_fn, kw=kw, h=h, hr=hr, nrg=nrg):
            r = Sched._dynregs.get(eng)
            if r is None:
                r = e.alloc_register(f"pdynreg_{eng}")
                Sched._dynregs[eng] = r
            e.reg_load(r, psb[0:1, pcol:pcol + 1])
            sv = e.snap(r, min_val=lo, max_val=hi)
            e.dma_start(out=out_fn(sv), in_=in_fn(sv), **kw).then_inc(h, 16)
            e.free_register(sv.val)
        self._app(eng, f)

    def dma_f(self, eng, out_fn, in_fn, sname, **kw):
        self.cnt[sname] += 16
        h = self.sem[sname]

        def f(e, out_fn=out_fn, in_fn=in_fn, kw=kw, h=h):
            e.dma_start(out=out_fn(), in_=in_fn(), **kw).then_inc(h, 16)
        self._app(eng, f)

    def wait(self, eng, sname, n=None):
        if n is None:
            n = self.cnt[sname]
        h = self.sem[sname]
        self.q[eng].append(lambda e, h=h, n=n: e.wait_ge(h, n))

    def mark(self, sname):
        return self.cnt[sname]



def build_nc():
    nc = bass.Bass("TRN2", target_bir_lowering=False, debug=False,
                   detect_race_conditions=False)
    s = Sched(nc)

    x0f = nc.dram_tensor("x0f", [B * T, D], F32, kind="ExternalInput").ap()
    params = nc.dram_tensor("params", [1, 128], I32, kind="ExternalInput").ap()
    wq_d = nc.dram_tensor("wq", [D, D], F32, kind="ExternalInput").ap()
    wkv_d = nc.dram_tensor("wkv", [D, LAT], F32, kind="ExternalInput").ap()
    wk_d = nc.dram_tensor("wk", [LAT, 512], F32, kind="ExternalInput").ap()
    wv_d = nc.dram_tensor("wv", [LAT, 512], F32, kind="ExternalInput").ap()
    wo_d = nc.dram_tensor("wo", [1024, D], F32, kind="ExternalInput").ap()
    qg_d = nc.dram_tensor("qg_rope", [512, 1], F32, kind="ExternalInput").ap()
    kg_d = nc.dram_tensor("kg", [512, 1], F32, kind="ExternalInput").ap()
    CONST_NAMES = ["ident_bf", "ident_f", "tri", "ind4", "bc4", "bc2",
                   "statx", "onesrow", "ind2", "bch2", "mleft", "mright",
                   "rotm"]
    CDT = {"ident_f": F32, "tri": F32, "mleft": F32, "mright": F32}
    cdram = {n: nc.dram_tensor(n, [128, 128], CDT.get(n, BF16),
                               kind="ExternalInput").ap() for n in CONST_NAMES}

    outs = {}
    if DEBUG_STAGE in (1, 2, 4, 5):
        outs["dbg_x1"] = nc.dram_tensor("dbg_x1", [2 * D, MB], F32,
                                        kind="ExternalOutput").ap()

    rope_d = {n: nc.dram_tensor(n, [128, 2048], F32, kind="ExternalInput").ap()
              for n in ("rope_cxs", "rope_sxs", "rope_cds", "rope_sds")}
    hk_d = nc.dram_tensor("halo_kill", [2, 128, 1], F32, kind="ExternalInput").ap()
    fwqkv_d = nc.dram_tensor("fwqkv", [2, 12, 8, 128, 128], F32, kind="ExternalInput").ap()
    fwo_d = nc.dram_tensor("fwo", [2, 8, 8, 128, 128], F32, kind="ExternalInput").ap()
    fw1a_d = nc.dram_tensor("fw1a", [2, 22, 8, 128, 128], F32, kind="ExternalInput").ap()
    fw1g_d = nc.dram_tensor("fw1g", [2, 22, 8, 128, 128], F32, kind="ExternalInput").ap()
    fw2_d = nc.dram_tensor("fw2", [2, 8, 22, 128, 128], F32, kind="ExternalInput").ap()
    fb1a_d = nc.dram_tensor("fb1a", [2, 128, 22], F32, kind="ExternalInput").ap()
    fb1g_d = nc.dram_tensor("fb1g", [2, 128, 22], F32, kind="ExternalInput").ap()
    fb2_d = nc.dram_tensor("fb2", [2, 128, 8], F32, kind="ExternalInput").ap()
    flng_d = nc.dram_tensor("flng", [2, 128, 8], F32, kind="ExternalInput").ap()
    flnb_d = nc.dram_tensor("flnb", [2, 128, 8], F32, kind="ExternalInput").ap()
    fqs_d = nc.dram_tensor("fqs", [2, 128, 1], F32, kind="ExternalInput").ap()
    fks_d = nc.dram_tensor("fks", [2, 128, 1], F32, kind="ExternalInput").ap()

    wbf_qkv = [nc.dram_tensor(f"wbf_qkv{i}", [12 * 8 * 128, 128], BF16, kind="Internal").ap() for i in range(2)]
    wbf_wo = [nc.dram_tensor(f"wbf_wo{i}", [8 * 8 * 128, 128], BF16, kind="Internal").ap() for i in range(2)]
    wbf_w1a = [nc.dram_tensor(f"wbf_w1a{i}", [22 * 8 * 128, 128], BF16, kind="Internal").ap() for i in range(2)]
    wbf_w1g = [nc.dram_tensor(f"wbf_w1g{i}", [22 * 8 * 128, 128], BF16, kind="Internal").ap() for i in range(2)]
    wbf_w2 = [nc.dram_tensor(f"wbf_w2{i}", [8 * 22 * 128, 128], BF16, kind="Internal").ap() for i in range(2)]
    QKVH_W = 4096 + 4096 + 4160
    qkvh = nc.dram_tensor("qkvh", [2 * 128, QKVH_W], BF16, kind="Internal").ap()
    ckv = nc.dram_tensor("ckv", [256, KVH_W], BF16, kind="Internal").ap()
    gkv0 = nc.dram_tensor("gkv0", [18 * 128, KVH_W], BF16, kind="Internal", addr_space="Shared").ap()
    gkv1 = nc.dram_tensor("gkv1", [18 * 128, KVH_W], BF16, kind="Internal", addr_space="Shared").ap()
    crow = nc.dram_tensor("crow", [256, 8], F32, kind="Internal").ap()
    grows = [nc.dram_tensor(f"grow{i}", [18 * 128, 8], F32, kind="Internal", addr_space="Shared").ap()
             for i in range(4)]
    out_t = nc.dram_tensor("out_t", [2 * MB, D], F32, kind="ExternalOutput").ap()

    x0bf = nc.dram_tensor("x0bf", [2 * MB, D], BF16, kind="Internal").ap()
    q_home = nc.dram_tensor("q_home", [2 * 512, MB], BF16, kind="Internal").ap()
    oh_home = nc.dram_tensor("oh_home", [2 * 1024, MB], F32, kind="Internal").ap()
    x1_spill = nc.dram_tensor("x1_spill", [2 * D, MB], F32, kind="Internal").ap()

    from contextlib import ExitStack
    _phase_guards = []

    with ExitStack() as _stack:
        def _salloc(name, shape, dt):
            g = nc.sbuf_tensor(name, shape, dt)
            h = g.__enter__()
            _phase_guards.append(g)
            return h

        def _free_phase():
            while _phase_guards:
                _phase_guards.pop().__exit__(None, None, None)
        c_ident_bf = _stack.enter_context(nc.sbuf_tensor("c_ident_bf", [128, 128], BF16))
        c_ident_f = _stack.enter_context(nc.sbuf_tensor("c_ident_f", [128, 128], F32))
        c_tri = _stack.enter_context(nc.sbuf_tensor("c_tri", [128, 128], F32))
        c_ind4 = _stack.enter_context(nc.sbuf_tensor("c_ind4", [128, 128], BF16))
        c_bc4 = _stack.enter_context(nc.sbuf_tensor("c_bc4", [128, 128], BF16))
        c_bc2 = _stack.enter_context(nc.sbuf_tensor("c_bc2", [128, 128], BF16))
        c_statx = _stack.enter_context(nc.sbuf_tensor("c_statx", [128, 128], BF16))
        c_ones = _stack.enter_context(nc.sbuf_tensor("c_ones", [128, 128], BF16))
        c_ind2 = _stack.enter_context(nc.sbuf_tensor("c_ind2", [128, 128], BF16))
        c_bch2 = _stack.enter_context(nc.sbuf_tensor("c_bch2", [128, 128], BF16))
        c_mleft = _stack.enter_context(nc.sbuf_tensor("c_mleft", [128, 128], F32))
        c_mright = _stack.enter_context(nc.sbuf_tensor("c_mright", [128, 128], F32))
        c_rotm = _stack.enter_context(nc.sbuf_tensor("c_rotm", [128, 128], BF16))
        c_qg = _stack.enter_context(nc.sbuf_tensor("c_qg", [128, 4], F32))
        c_kg = _stack.enter_context(nc.sbuf_tensor("c_kg", [128, 4], F32))
        c_eps = _stack.enter_context(nc.sbuf_tensor("c_eps", [128, 2], F32))
        p_sb = _stack.enter_context(nc.sbuf_tensor("p_sb", [1, 128], I32))
        scr_f = _stack.enter_context(nc.sbuf_tensor("scr_f", [128, MB], F32))
        scr2_f = _stack.enter_context(nc.sbuf_tensor("scr2_f", [128, 512], F32))
        st_f = _stack.enter_context(nc.sbuf_tensor("st_f", [128, MB], F32))
        st_bf = _stack.enter_context(nc.sbuf_tensor("st_bf", [128, MB], BF16))
        work = _stack.enter_context(nc.sbuf_tensor("work", [128, 8 * MB + 8], F32))
        x0own = _stack.enter_context(nc.sbuf_tensor("x0own", [128, 8 * MB + 8], F32))
        ps_sc = _stack.enter_context(nc.psum_tensor("ps_sc", [128, 2048], F32))
        ps_av0 = _stack.enter_context(nc.psum_tensor("ps_av0", [128, 512], F32))
        ps_av1 = _stack.enter_context(nc.psum_tensor("ps_av1", [128, 512], F32))
        ps_m = _stack.enter_context(nc.psum_tensor("ps_m", [128, 512], F32))
        ps_t = _stack.enter_context(nc.psum_tensor("ps_t", [128, 512], F32))
        w_q = _salloc("w_q", [128, 8 * 1024], BF16)
        w_kv = _salloc("w_kv", [128, 8 * 128], BF16)
        w_k = _salloc("w_k", [128, 512], BF16)
        w_v = _salloc("w_v", [128, 512], BF16)
        w_o = _salloc("w_o", [128, 8 * 1024], BF16)
        x0fm = _salloc("x0fm", [128, 8 * MB], BF16)
        kv_sb = _salloc("kv_sb", [128, MB], BF16)
        qraw = _salloc("qraw", [128, 4 * MB], F32)
        kp_sb = _salloc("kp_sb", [128, 4 * MB], BF16)
        vp_sb = _salloc("vp_sb", [128, 8 * 544], BF16)
        qp_sb = _salloc("qp_sb", [128, 4 * MB], BF16)
        pt_sb = _salloc("pt_sb", [128, 2048], BF16)
        inv_bf = _salloc("inv_bf", [128, MB], BF16)
        outn = _salloc("outn", [128, 8 * MB], BF16)
        xtm = _salloc("xtm", [128, MB], F32)
        block = _stack.enter_context(nc.Block())

        s.new_sems("sWL", "sCast", "sTr", "sMM", "sSC", "sVE", "sST",
                   "sQD", "sOD", "sCC", "sRG", "sGH")
        consts = {"ident_bf": c_ident_bf, "ident_f": c_ident_f, "tri": c_tri,
                  "ind4": c_ind4, "bc4": c_bc4, "bc2": c_bc2,
                  "statx": c_statx, "onesrow": c_ones, "ind2": c_ind2,
                  "bch2": c_bch2, "mleft": c_mleft, "mright": c_mright,
                  "rotm": c_rotm}

        H = {}

        WW = work.shape[1]
        QW = qp_sb.shape[1]
        OW = outn.shape[1]
        XW = x0own.shape[1]

        s.dma("sync", p_sb[:], params, "sWL")
        for n in CONST_NAMES:
            s.dma("sync", consts[n][:], cdram[n], "sWL")
        for g in range(4):
            s.dma("sync", c_qg[:, g:g + 1], qg_d[128 * g:128 * (g + 1), :], "sWL")
            s.dma("sync", c_kg[:, g:g + 1], kg_d[128 * g:128 * (g + 1), :], "sWL")
        for kc in range(8):
            s.dma("gpsimd", w_q[:, kc * 1024:(kc + 1) * 1024],
                  wq_d[128 * kc:128 * (kc + 1), :], "sCast")
            s.dma("gpsimd", w_kv[:, kc * 128:(kc + 1) * 128],
                  wkv_d[128 * kc:128 * (kc + 1), :], "sCast")
            s.dma("gpsimd", w_o[:, kc * 1024:(kc + 1) * 1024],
                  wo_d[128 * kc:128 * (kc + 1), :], "sCast")
        s.dma("gpsimd", w_k[:], wk_d, "sCast")
        s.dma("gpsimd", w_v[:], wv_d, "sCast")
        mk_weights = s.mark("sCast")

        s.op("vector", lambda e: e.memset(c_eps[:, 0:1], 1e-6))
        s.op("vector", lambda e: e.memset(c_eps[:, 1:2], 1e-5))
        s.op("vector", lambda e: e.memset(work[:], 0.0))
        s.op("vector", lambda e: e.memset(outn[:], 0.0))
        s.op("vector", lambda e: e.memset(vp_sb[:], 0.0))
        s.op("vector", lambda e: e.memset(inv_bf[:], 0.0))
        s.op("vector", lambda e: e.memset(ps_sc[:], 0.0))
        s.op("vector", lambda e: e.memset(ps_av0[:], 0.0))
        s.op("vector", lambda e: e.memset(ps_av1[:], 0.0))
        s.op("vector", lambda e: e.memset(ps_m[:], 0.0))
        s.op("vector", lambda e: e.memset(ps_t[:], 0.0))
        for t8 in range(8):
            s.op("vector", lambda e, t8=t8: e.memset(
                bass.AP(vp_sb, t8 * 544 + 32, [[vp_sb.shape[1], 128], [33, 16], [1, 1]]),
                1.0))
        s.op("vector", lambda e: e.memset(pt_sb[:], 0.0), "sVE")

        s.wait("sync", "sVE")
        for blk in range(16):
            s.dma("sync", oh_home[128 * blk:128 * (blk + 1), :], work[:, 0:MB], "sOD")
        s.wait("sync", "sWL", 16)
        s.wait("gpsimd", "sWL", 16)

        def load_x0fm(poff, parity):
            s.wait("gpsimd", "sTr")
            s.pdyn("gpsimd", p_sb, poff, 0, (B * T - MB) * D,
                   lambda sv, parity=parity: x0bf[parity * MB:(parity + 1) * MB, :],
                   lambda sv: bass.AP(x0f.tensor, sv, [[D, MB], [1, D]]),
                   "sCast")
            s.wait("sync", "sCast")
            s.wait("sync", "sMM")
            for c in range(8):
                s.dma("sync", x0fm[:, c * MB:(c + 1) * MB],
                      x0bf[parity * MB:(parity + 1) * MB, 128 * c:128 * (c + 1)],
                      "sTr", transpose=True)

        def kv_chain():
            mk_tr = s.mark("sTr")
            for tt in range(2):
                s.wait("tensor", "sTr", mk_tr)
                s.wait("tensor", "sVE")
                s.wait("tensor", "sSC")

                def fkv(e, tt=tt):
                    for kc in range(8):
                        ins = e.matmul(ps_m[:, :], w_kv[:, kc * 128:(kc + 1) * 128],
                                       x0fm[:, kc * MB + tt * 512: kc * MB + tt * 512 + 512],
                                       start=(kc == 0), stop=(kc == 7))
                    return ins
                s.op("tensor", fkv, "sMM")
                s.wait("vector", "sMM")
                s.op("vector", lambda e, tt=tt: e.tensor_copy(
                    kv_sb[:, tt * 512:tt * 512 + 512], ps_m[:, :]), "sVE")

            for oc in range(4):
                for tt in range(2):
                    c0 = tt * 512
                    s.wait("tensor", "sVE")
                    s.wait("tensor", "sSC")
                    s.op("tensor", lambda e, oc=oc, c0=c0: e.matmul(
                        ps_m[:, :], w_k[:, oc * 128:(oc + 1) * 128],
                        kv_sb[:, c0:c0 + 512], start=True, stop=True), "sMM")
                    s.wait("scalar", "sMM")
                    s.op("scalar", lambda e: e.activation(
                        st_bf[:, 0:512], ps_m[:, :], AF.Square), "sSC")
                    s.wait("tensor", "sSC")
                    s.wait("tensor", "sVE")
                    s.op("tensor", lambda e: e.matmul(
                        ps_t[:, 0:512], c_ind4[:], st_bf[:, 0:512],
                        start=True, stop=True), "sMM")
                    s.wait("scalar", "sMM")
                    s.op("scalar", lambda e: e.activation(
                        st_f[0:4, 0:512], ps_t[0:4, 0:512], AF.Sqrt,
                        bias=c_eps[0:4, 0:1], scale=1.0 / 32), "sSC")
                    s.wait("vector", "sSC")
                    s.op("vector", lambda e: e.reciprocal(
                        st_f[0:4, 512:1024], st_f[0:4, 0:512]))
                    s.op("vector", lambda e: e.tensor_copy(
                        st_bf[0:4, 512:1024], st_f[0:4, 512:1024]), "sVE")
                    s.wait("tensor", "sVE")
                    s.op("tensor", lambda e: e.matmul(
                        ps_t[:, 0:512], c_bc4[0:4, :], st_bf[0:4, 512:1024],
                        start=True, stop=True), "sMM")
                    s.wait("vector", "sMM")
                    s.op("vector", lambda e, oc=oc: e.tensor_scalar(
                        scr_f[:, 0:512], ps_m[:, :], c_kg[:, oc:oc + 1],
                        None, ALU.mult))
                    s.op("vector", lambda e, oc=oc, c0=c0: e.tensor_mul(
                        kp_sb[:, oc * MB + c0: oc * MB + c0 + 512],
                        scr_f[:, 0:512], ps_t[:, 0:512]), "sVE")

            for t8 in range(8):
                s.wait("tensor", "sVE")
                s.wait("tensor", "sSC")
                s.op("tensor", lambda e, t8=t8: e.matmul(
                    ps_m[:, :], kv_sb[:, t8 * 128:(t8 + 1) * 128], w_v[:],
                    start=True, stop=True), "sMM")
                s.wait("vector", "sMM")
                s.op("vector", lambda e, t8=t8: e.tensor_copy(
                    bass.AP(vp_sb, t8 * 544, [[vp_sb.shape[1], 128], [33, 16], [1, 32]]),
                    bass.AP(ps_m, 0, [[512, 128], [32, 16], [1, 32]])),
                    "sVE")

        def mla_attend(diag):
            for hg in range(4):
                for qb in range(2):
                    if diag is None:
                        scs = [(sc, 0) for sc in range(8)]
                    elif qb == 0:
                        scs = [(c, 128 * c) for c in range(4)]
                    else:
                        scs = [(c, 0) for c in range(4)] + \
                              [(c, 128 * (c - 4)) for c in range(4, 8)]
                    nsc = len(scs)
                    for idx, (sc, qlo) in enumerate(scs):
                        has_tri = diag is not None and (
                            (qb == 0) or (qb == 1 and sc >= 4))
                        first = (idx == 0)
                        last = (idx == nsc - 1)
                        s.wait("tensor", "sSC")
                        s.wait("tensor", "sVE")

                        def fqkt(e, hg=hg, qb=qb, sc=sc, qlo=qlo):
                            for h4 in range(4):
                                ins = e.matmul(
                                    ps_sc[:, 512 * h4 + qlo: 512 * h4 + 512],
                                    kp_sb[32 * h4:32 * h4 + 32,
                                          hg * MB + sc * 128: hg * MB + sc * 128 + 128],
                                    qp_sb[32 * h4:32 * h4 + 32,
                                          hg * MB + qb * 512 + qlo: hg * MB + qb * 512 + 512],
                                    start=True, stop=True, tile_position=(32 * h4, 0))
                            return ins
                        s.op("tensor", fqkt, "sMM")
                        if has_tri:
                            s.wait("vector", "sMM")

                            def fmask(e, qlo=qlo):
                                for h4 in range(4):
                                    ins = e.tensor_add(
                                        ps_sc[:, 512 * h4 + qlo: 512 * h4 + qlo + 128],
                                        ps_sc[:, 512 * h4 + qlo: 512 * h4 + qlo + 128],
                                        c_tri[:])
                                return ins
                            s.op("vector", fmask, "sVE")
                            s.wait("scalar", "sVE")
                        s.wait("scalar", "sMM")
                        s.op("scalar", lambda e: e.activation(
                            pt_sb[:], ps_sc[:], AF.Exp), "sSC")
                        s.wait("tensor", "sSC")

                        def fav(e, hg=hg, sc=sc, qlo=qlo, first=first, last=last):
                            for pr, bank in ((0, ps_av0), (1, ps_av1)):
                                for hh in range(2):
                                    h4 = 2 * pr + hh
                                    h = hg * 4 + h4
                                    rows = bank[0:33, qlo:512] if hh == 0 else bank[64:97, qlo:512]
                                    ins = e.matmul(
                                        rows,
                                        vp_sb[:, sc * 544 + 33 * h - (hg * 4 * 33) + (hg * 4 * 33): sc * 544 + 33 * h4 + 33]
                                        if False else
                                        vp_sb[:, sc * 544 + 33 * h: sc * 544 + 33 * h + 33],
                                        pt_sb[:, 512 * h4 + qlo: 512 * h4 + 512],
                                        start=first, stop=last,
                                        tile_position=(0, 0 if hh == 0 else 64),
                                        skip_group_check=True)
                            return ins
                        s.op("tensor", fav, "sMM")
                    s.wait("vector", "sMM")
                    for pr, bank in ((0, ps_av0), (1, ps_av1)):
                        pp = hg * 2 + pr
                        col = pp * MB + qb * 512
                        s.op("vector", lambda e, col=col, bank=bank: e.tensor_add(
                            bass.AP(work, col, [[WW, 33], [1, 512]]),
                            bass.AP(work, col, [[WW, 33], [1, 512]]),
                            bass.AP(bank, 0, [[512, 33], [1, 512]])))
                        s.op("vector", lambda e, col=col, bank=bank: e.tensor_add(
                            bass.AP(work, 64 * WW + col, [[WW, 33], [1, 512]]),
                            bass.AP(work, 64 * WW + col, [[WW, 33], [1, 512]]),
                            bass.AP(bank, 64 * 512, [[512, 33], [1, 512]])),
                            "sVE")

        def build_q(ch):
            import os
            sub = int(os.environ.get("K_SUB", "9"))
            load_x0fm(40 + ch, ch % 2)
            if sub < 1:
                return
            s.wait("vector", "sQD")
            mk_tr = s.mark("sTr")
            for g in range(4):
                for tt in range(2):
                    c0 = tt * 512

                    def fq(e, g=g, c0=c0, nr=0):
                        oc = g + 4 * nr
                        for kc in range(8):
                            ins = e.matmul(ps_m[:, :],
                                           w_q[:, kc * 1024 + oc * 128: kc * 1024 + oc * 128 + 128],
                                           x0fm[:, kc * MB + c0: kc * MB + c0 + 512],
                                           start=(kc == 0), stop=(kc == 7))
                        return ins
                    s.wait("tensor", "sTr", mk_tr)
                    s.wait("tensor", "sCast", mk_weights)
                    s.wait("tensor", "sVE")
                    s.wait("tensor", "sSC")
                    s.op("tensor", lambda e, g=g, c0=c0: fq(e, g, c0, 0), "sMM")
                    if sub < 2:
                        continue
                    if sub != 21:
                        s.wait("scalar", "sMM")
                        s.op("scalar", lambda e: e.activation(
                            st_bf[:, 0:512], ps_m[:, :], AF.Square), "sSC")
                    if sub == 20:
                        continue
                    s.wait("vector", "sMM")
                    s.wait("vector", "sSC")
                    s.op("vector", lambda e, g=g, c0=c0: e.tensor_copy(
                        qraw[:, g * MB + c0: g * MB + c0 + 512], ps_m[:, :]), "sVE")
                    if sub < 3 or sub in (21,):
                        continue
                    s.wait("tensor", "sSC")
                    s.wait("tensor", "sVE")
                    s.op("tensor", lambda e, g=g, c0=c0: fq(e, g, c0, 1), "sMM")
                    s.wait("scalar", "sMM")
                    s.op("scalar", lambda e: e.activation(
                        st_bf[:, 512:1024], ps_m[:, :], AF.Square), "sSC")
                    s.wait("tensor", "sSC")
                    s.wait("tensor", "sVE")

                    def fss(e):
                        e.matmul(ps_t[:, 0:512], c_ind4[:], st_bf[:, 0:512],
                                 start=True, stop=False)
                        return e.matmul(ps_t[:, 0:512], c_ind4[:], st_bf[:, 512:1024],
                                        start=False, stop=True)
                    s.op("tensor", fss, "sMM")
                    if sub < 4:
                        continue
                    s.wait("scalar", "sMM")
                    s.op("scalar", lambda e: e.activation(
                        st_f[0:4, 0:512], ps_t[0:4, 0:512], AF.Sqrt,
                        bias=c_eps[0:4, 0:1], scale=1.0 / 64), "sSC")
                    s.wait("vector", "sSC")
                    s.op("vector", lambda e: e.reciprocal(
                        st_f[0:4, 512:1024], st_f[0:4, 0:512]))
                    s.op("vector", lambda e: e.tensor_copy(
                        st_bf[0:4, 512:1024], st_f[0:4, 512:1024]), "sVE")
                    if sub < 5:
                        continue
                    s.wait("tensor", "sVE")
                    s.op("tensor", lambda e: e.matmul(
                        ps_t[:, 0:512], c_bc4[0:4, :], st_bf[0:4, 512:1024],
                        start=True, stop=True), "sMM")
                    if sub < 6:
                        continue
                    s.wait("vector", "sMM")
                    s.op("vector", lambda e, g=g, c0=c0: e.tensor_scalar(
                        scr_f[:, 0:512], qraw[:, g * MB + c0: g * MB + c0 + 512],
                        c_qg[:, g:g + 1], 0.125, ALU.mult, ALU.mult))
                    s.op("vector", lambda e, g=g, c0=c0: e.tensor_mul(
                        qp_sb[:, g * MB + c0: g * MB + c0 + 512],
                        scr_f[:, 0:512], ps_t[:, 0:512]), "sVE")
            if sub < 7:
                return
            s.wait("sync", "sVE")
            s.dma("sync",
                  bass.AP(q_home.tensor, ch * 512 * MB,
                          [[4 * MB, 128], [MB, 4], [1, MB]]),
                  bass.AP(qp_sb, 0, [[QW, 128], [MB, 4], [1, MB]]), "sQD")

        def mla_iter(j):
            parity = j % 2
            load_x0fm(4 * j, parity)
            s.wait("sync", "sVE")
            s.wait("sync", "sOD")
            s.wait("sync", "sQD")
            s.pdyn("sync", p_sb, 4 * j + 1, 0, 512 * MB,
                   lambda sv: bass.AP(qp_sb, 0, [[QW, 128], [MB, 4], [1, MB]]),
                   lambda sv: bass.AP(q_home.tensor, sv,
                                      [[4 * MB, 128], [MB, 4], [1, MB]]),
                   "sQD")
            s.pdyn("sync", p_sb, 4 * j + 2, 0, 1024 * MB,
                   lambda sv: bass.AP(work, 0, [[WW, 128], [MB, 8], [1, MB]]),
                   lambda sv: bass.AP(oh_home.tensor, sv,
                                      [[8 * MB, 128], [MB, 8], [1, MB]]),
                   "sOD")
            kv_chain()
            s.wait("tensor", "sQD")
            s.wait("vector", "sOD")
            diag = "A" if j == 0 else ("B" if j == 8 else None)
            mla_attend(diag)
            s.wait("sync", "sVE")
            s.pdyn("sync", p_sb, 4 * j + 2, 0, 1024 * MB,
                   lambda sv: bass.AP(oh_home.tensor, sv,
                                      [[8 * MB, 128], [MB, 8], [1, MB]]),
                   lambda sv: bass.AP(work, 0, [[WW, 128], [MB, 8], [1, MB]]),
                   "sOD")

        def mla_finish(ch):
            s.wait("sync", "sOD")
            s.dma("sync",
                  bass.AP(work, 0, [[WW, 128], [MB, 8], [1, MB]]),
                  bass.AP(oh_home.tensor, ch * 1024 * MB,
                          [[8 * MB, 128], [MB, 8], [1, MB]]),
                  "sOD")
            QRW = qraw.shape[1]
            for g in range(2):
                s.wait("sync", "sMM")
                s.pdyn("sync", p_sb, 54 + ch * 8 + 4 * g, 0, (B * T - 512) * D,
                       lambda sv: bass.AP(qraw, 0, [[QRW, 128], [1024, 4], [1, 1024]]),
                       lambda sv: bass.AP(x0f.tensor, sv,
                                          [[D, 128], [128 * D, 4], [1, D]]),
                       "sTr")
                mk_x = s.mark("sTr")
                for t4 in range(4):
                    t8 = 4 * g + t4
                    for c in range(8):
                        s.wait("tensor", "sTr", mk_x)
                        s.wait("tensor", "sVE")
                        s.op("tensor", lambda e, c=c, t4=t4: e.transpose(
                            ps_t[:, 0:128],
                            qraw[:, t4 * 1024 + c * 128: t4 * 1024 + c * 128 + 128],
                            c_ident_f[:]), "sMM")
                        s.wait("vector", "sMM")
                        s.op("vector", lambda e, c=c, t8=t8: e.tensor_copy(
                            x0own[:, c * MB + t8 * 128: c * MB + t8 * 128 + 128],
                            ps_t[:, 0:128]), "sVE")

            s.wait("vector", "sOD")
            for pp in range(8):
                col = pp * MB
                SW = st_f.shape[1]
                IW = inv_bf.shape[1]
                s.op("vector", lambda e, col=col, SW=SW: e.reciprocal(
                    bass.AP(st_f, 32 * SW, [[SW, 1], [1, MB]]),
                    bass.AP(work, 32 * WW + col, [[WW, 1], [1, MB]])))
                s.op("vector", lambda e, col=col, SW=SW: e.reciprocal(
                    bass.AP(st_f, 64 * SW, [[SW, 33], [1, MB]]),
                    bass.AP(work, 64 * WW + col, [[WW, 33], [1, MB]])))
                s.op("vector", lambda e, SW=SW, IW=IW: e.tensor_copy(
                    bass.AP(inv_bf, 32 * IW, [[IW, 1], [1, MB]]),
                    bass.AP(st_f, 32 * SW, [[SW, 1], [1, MB]])))
                s.op("vector", lambda e, SW=SW, IW=IW: e.tensor_copy(
                    bass.AP(inv_bf, 64 * IW, [[IW, 33], [1, MB]]),
                    bass.AP(st_f, 64 * SW, [[SW, 33], [1, MB]])))
                s.op("vector", lambda e, IW=IW: e.memset(
                    bass.AP(inv_bf, 64 * IW, [[IW, 32], [1, MB]]), 0.0), "sVE")
                for tt in range(2):
                    s.wait("tensor", "sVE")
                    s.wait("tensor", "sSC")
                    s.op("tensor", lambda e, tt=tt: e.matmul(
                        ps_t[:, 0:512], c_bc2[:], inv_bf[:, tt * 512:(tt + 1) * 512],
                        start=True, stop=True), "sMM")
                    s.wait("vector", "sMM")
                    s.op("vector", lambda e, col=col, tt=tt: e.tensor_mul(
                        bass.AP(outn, col + tt * 512, [[OW, 32], [1, 512]]),
                        bass.AP(work, col + tt * 512, [[WW, 32], [1, 512]]),
                        bass.AP(ps_t, 0, [[512, 32], [1, 512]])))
                    s.op("vector", lambda e, col=col, tt=tt: e.tensor_mul(
                        bass.AP(outn, 64 * OW + col + tt * 512, [[OW, 32], [1, 512]]),
                        bass.AP(work, 64 * WW + col + tt * 512, [[WW, 32], [1, 512]]),
                        bass.AP(ps_t, 64 * 512, [[512, 32], [1, 512]])),
                        "sVE")

            for tt in range(2):
                for oc in range(8):
                    s.wait("tensor", "sVE")
                    s.wait("tensor", "sSC")

                    def fa9(e, tt=tt, oc=oc):
                        for pp in range(8):
                            ins = e.matmul(
                                ps_m[:, :],
                                w_o[:, pp * 1024 + oc * 128: pp * 1024 + oc * 128 + 128],
                                outn[:, pp * MB + tt * 512: pp * MB + tt * 512 + 512],
                                start=(pp == 0), stop=(pp == 7))
                        return ins
                    s.op("tensor", fa9, "sMM")
                    s.wait("vector", "sMM")
                    s.op("vector", lambda e, tt=tt, oc=oc: e.tensor_add(
                        x0own[:, oc * MB + tt * 512: oc * MB + tt * 512 + 512],
                        x0own[:, oc * MB + tt * 512: oc * MB + tt * 512 + 512],
                        ps_m[:, :]), "sVE")

            for tt in range(2):
                s.wait("scalar", "sVE")
                for oc in range(8):
                    inc = "sVE" if oc == 7 else None
                    s.op("vector", lambda e, tt=tt, oc=oc: e.tensor_copy(
                        x0fm[:, oc * 512: oc * 512 + 512],
                        x0own[:, oc * MB + tt * 512: oc * MB + tt * 512 + 512]),
                        inc)
                    s.op("scalar", lambda e, tt=tt, oc=oc: e.activation(
                        x0fm[:, 4096 + oc * 512: 4096 + oc * 512 + 512],
                        x0own[:, oc * MB + tt * 512: oc * MB + tt * 512 + 512],
                        AF.Square), "sSC" if oc == 7 else None)
                s.wait("tensor", "sVE")
                s.wait("tensor", "sSC")

                def fstat(e):
                    for oc in range(8):
                        e.matmul(ps_t[:, 0:512], c_statx[:],
                                 x0fm[:, oc * 512: oc * 512 + 512],
                                 start=(oc == 0), stop=(oc == 7))
                    for oc in range(8):
                        ins = e.matmul(ps_m[:, 0:512], c_statx[:],
                                       x0fm[:, 4096 + oc * 512: 4096 + oc * 512 + 512],
                                       start=(oc == 0), stop=(oc == 7))
                    return ins
                s.op("tensor", fstat, "sMM")
                s.wait("vector", "sMM")
                s.op("vector", lambda e: e.tensor_scalar(
                    st_f[0:1, 0:512], ps_t[0:1, 0:512], 1.0 / 1024, None, ALU.mult))
                s.op("vector", lambda e: e.tensor_scalar(
                    st_f[0:1, 512:1024], ps_m[0:1, 0:512], 1.0 / 1024, None, ALU.mult))
                s.op("vector", lambda e: e.tensor_mul(
                    scr2_f[0:1, 0:512], st_f[0:1, 0:512], st_f[0:1, 0:512]))
                s.op("vector", lambda e: e.tensor_sub(
                    scr2_f[0:1, 0:512], st_f[0:1, 512:1024], scr2_f[0:1, 0:512]), "sVE")
                s.wait("scalar", "sVE")
                s.op("scalar", lambda e: e.activation(
                    scr2_f[0:1, 0:512], scr2_f[0:1, 0:512], AF.Sqrt,
                    bias=c_eps[0:1, 1:2]), "sSC")
                s.wait("vector", "sSC")
                s.op("vector", lambda e: e.reciprocal(
                    st_f[0:1, 512:1024], scr2_f[0:1, 0:512]))
                s.op("vector", lambda e: e.tensor_copy(
                    st_bf[0:1, 0:512], st_f[0:1, 512:1024]))
                s.op("vector", lambda e: e.tensor_sub(
                    scr2_f[0:1, 0:512], st_f[0:1, 512:1024], st_bf[0:1, 0:512]))
                s.op("vector", lambda e: e.tensor_copy(
                    st_bf[0:1, 512:1024], scr2_f[0:1, 0:512]))
                s.op("vector", lambda e: e.tensor_scalar(
                    st_f[0:1, 0:512], st_f[0:1, 0:512], -1.0, None, ALU.mult))
                s.op("vector", lambda e: e.tensor_copy(
                    kv_sb[0:1, 0:512], st_f[0:1, 0:512]))
                s.op("vector", lambda e: e.tensor_sub(
                    scr2_f[0:1, 0:512], st_f[0:1, 0:512], kv_sb[0:1, 0:512]))
                s.op("vector", lambda e: e.tensor_copy(
                    kv_sb[0:1, 512:1024], scr2_f[0:1, 0:512]), "sVE")
                s.wait("tensor", "sVE")

                def fbc(e):
                    e.matmul(ps_t[:, 0:512], c_ones[0:1, :], kv_sb[0:1, 0:512],
                             start=True, stop=False)
                    e.matmul(ps_t[:, 0:512], c_ones[0:1, :], kv_sb[0:1, 512:1024],
                             start=False, stop=True)
                    e.matmul(ps_m[:, 0:512], c_ones[0:1, :], st_bf[0:1, 0:512],
                             start=True, stop=False)
                    return e.matmul(ps_m[:, 0:512], c_ones[0:1, :], st_bf[0:1, 512:1024],
                                    start=False, stop=True)
                s.op("tensor", fbc, "sMM")
                s.wait("vector", "sMM")
                for oc in range(8):
                    s.op("vector", lambda e, tt=tt, oc=oc: e.tensor_add(
                        scr_f[:, 0:512],
                        x0own[:, oc * MB + tt * 512: oc * MB + tt * 512 + 512],
                        ps_t[:, 0:512]))
                    s.op("vector", lambda e, tt=tt, oc=oc: e.tensor_mul(
                        x0own[:, oc * MB + tt * 512: oc * MB + tt * 512 + 512],
                        scr_f[:, 0:512], ps_m[:, 0:512]), "sVE")

            s.wait("sync", "sVE")
            for oc in range(8):
                s.dma("sync", x1_spill[ch * D + oc * 128: ch * D + (oc + 1) * 128, :],
                      bass.AP(x0own, oc * MB, [[XW, 128], [1, MB]]), "sQD")
                if DEBUG_STAGE == 1:
                    s.dma("sync", outs["dbg_x1"][ch * D + oc * 128: ch * D + (oc + 1) * 128, :],
                          bass.AP(x0own, oc * MB, [[XW, 128], [1, MB]]), "sST")


        XK = MB + 1

        def barrier():
            for eng in ENGS:
                for sn in ("sMM", "sSC", "sVE", "sTr", "sWL",
                           "sQD", "sOD", "sST", "sCC", "sGH"):
                    s.wait(eng, sn)

        def allgather(cin, gout):
            s.wait("gpsimd", "sQD")
            s.wait("gpsimd", "sCC")
            s.op("gpsimd", lambda e, cin=cin, gout=gout: e.collective_compute(
                "AllGather", mybir.AluOpType.bypass,
                replica_groups=[list(range(8))],
                ins=[cin], outs=[gout]), "sCC")

        def emit_ffn():
            _free_phase()
            barrier()
            ffn_casts(0)
            ffn_casts(1)
            xc = [work, x0own]
            XCW = work.shape[1]
            tA = _salloc("tA", [128, 2048], BF16)
            tB = _salloc("tB", [128, 2048], BF16)
            tAk = _salloc("tAk", [128, 2048], BF16)
            tBk = _salloc("tBk", [128, 2048], BF16)
            qpl = _salloc("qpl", [128, 4 * 1024], BF16)
            kpl = _salloc("kpl", [128, 4 * 1536], BF16)
            vpl = _salloc("vpl", [128, 12 * 520], BF16)
            hbf = _salloc("hbf", [128, 4096], BF16)
            hsq = _salloc("hsq", [128, 4096], BF16)
            cpar = _salloc("cpar", [128, 80], F32)
            KW = kpl.shape[1]
            VW = vpl.shape[1]
            QPW = qpl.shape[1]
            ffn_guards = list(_phase_guards)

            s.op("vector", lambda e: e.memset(vpl[:], 0.0))
            for tau in range(12):
                s.op("vector", lambda e, tau=tau: e.memset(
                    bass.AP(vpl, tau * 520 + 64, [[vpl.shape[1], 128], [65, 8], [1, 1]]),
                    1.0), "sVE" if tau == 11 else None)
            for ch in range(2):
                for oc in range(8):
                    s.dma("sync",
                          bass.AP(xc[ch], oc * XK + 1, [[XCW, 128], [1, MB]]),
                          x1_spill[ch * D + oc * 128: ch * D + (oc + 1) * 128, :],
                          "sQD")
            s.op("vector", lambda e: e.memset(hbf[:], 0.0))
            s.op("vector", lambda e: e.memset(hsq[:], 0.0), "sVE")
            s.wait("sync", "sVE")
            for g in (gkv0, gkv1):
                s.dma("sync", g[16 * 128:17 * 128, 0:2048], hbf[:, 0:2048], "sQD")
                s.dma("sync", g[16 * 128:17 * 128, 2048:4096], hbf[:, 0:2048], "sQD")
                s.dma("sync", g[16 * 128:17 * 128, 4096:KVH_W], hbf[:, 0:KVH_W - 4096], "sQD")
            s.op("vector", lambda e: e.memset(scr_f[:, 0:8], 0.0), "sVE")
            s.wait("sync", "sVE")
            for g in grows:
                s.dma("sync", g[16 * 128:17 * 128, :],
                      bass.AP(scr_f, 0, [[0, 128], [1, 8]]) if False else scr_f[0:128, 0:8],
                      "sQD")

            import os
            nl = int(os.environ.get("K_NLAYER", "2"))
            for l in range(nl):
                emit_ffn_layer(l, xc, XCW, tA, tB, tAk, tBk, qpl, kpl, vpl,
                               hbf, hsq, cpar, KW, VW, QPW, ffn_guards)
            if DEBUG_STAGE == 2:
                barrier()
                for ch in range(2):
                    for oc in range(8):
                        s.dma("sync",
                              outs["dbg_x1"][ch * D + oc * 128: ch * D + (oc + 1) * 128, :],
                              bass.AP(xc[ch], oc * XK + 1, [[XCW, 128], [1, MB]]),
                              "sST")
            if int(os.environ.get("K_FFN", "9")) >= 9 and nl == 2 and DEBUG_STAGE != 2:
                emit_stage_d(xc, XCW, hbf, hsq)
                if DEBUG_STAGE == 4:
                    for ch in range(2):
                        for oc in range(8):
                            s.dma("sync",
                                  outs["dbg_x1"][1024 + ch * 512 + oc * 64:
                                                 1024 + ch * 512 + oc * 64 + 64, :],
                                  bass.AP(xc[ch], oc * XK + 1, [[XCW, 64], [1, MB]]),
                                  "sST")

        cast_cnt = {}

        def ffn_casts(l):
            s.dma("gpsimd", wbf_qkv[l],
                  bass.AP(fwqkv_d.tensor, l * 12 * 8 * 128 * 128,
                          [[128, 12 * 8 * 128], [1, 128]]), "sCast")
            s.dma("gpsimd", wbf_wo[l],
                  bass.AP(fwo_d.tensor, l * 8 * 8 * 128 * 128,
                          [[128, 8 * 8 * 128], [1, 128]]), "sCast")
            s.dma("gpsimd", wbf_w1a[l],
                  bass.AP(fw1a_d.tensor, l * 22 * 8 * 128 * 128,
                          [[128, 22 * 8 * 128], [1, 128]]), "sCast")
            s.dma("gpsimd", wbf_w1g[l],
                  bass.AP(fw1g_d.tensor, l * 22 * 8 * 128 * 128,
                          [[128, 22 * 8 * 128], [1, 128]]), "sCast")
            s.dma("gpsimd", wbf_w2[l],
                  bass.AP(fw2_d.tensor, l * 8 * 22 * 128 * 128,
                          [[128, 8 * 22 * 128], [1, 128]]), "sCast")
            cast_cnt[l] = s.cnt["sCast"]

        def ffn_prep(l, tA, tB, tAk, tBk, cpar, wsb, wob, rtab):
            s.wait("sync", "sVE")
            s.wait("sync", "sSC")
            s.dma("sync", cpar[:, 0:8],
                  bass.AP(flng_d.tensor, l * 128 * 8, [[8, 128], [1, 8]]), "sWL")
            s.dma("sync", cpar[:, 8:16],
                  bass.AP(flnb_d.tensor, l * 128 * 8, [[8, 128], [1, 8]]), "sWL")
            s.dma("sync", cpar[:, 16:24],
                  bass.AP(fb2_d.tensor, l * 128 * 8, [[8, 128], [1, 8]]), "sWL")
            s.dma("sync", cpar[:, 24:46],
                  bass.AP(fb1a_d.tensor, l * 128 * 22, [[22, 128], [1, 22]]), "sWL")
            s.dma("sync", cpar[:, 46:68],
                  bass.AP(fb1g_d.tensor, l * 128 * 22, [[22, 128], [1, 22]]), "sWL")
            s.dma("sync", cpar[:, 68:69],
                  bass.AP(fqs_d.tensor, l * 128, [[1, 128], [1, 1]]), "sWL")
            s.dma("sync", cpar[:, 69:70],
                  bass.AP(fks_d.tensor, l * 128, [[1, 128], [1, 1]]), "sWL")
            s.dma("sync", cpar[:, 70:71],
                  bass.AP(hk_d.tensor, 0, [[1, 128], [1, 1]]), "sWL")
            s.dma("sync", cpar[:, 71:72],
                  bass.AP(hk_d.tensor, 128, [[1, 128], [1, 1]]), "sWL")
            for (dn, tdst) in (("rope_cxs", tA), ("rope_sxs", tB),
                               ("rope_cds", tAk), ("rope_sds", tBk)):
                s.wait("sync", "sVE")
                s.dma("sync", rtab[:], rope_d[dn], "sWL")
                s.wait("vector", "sWL")
                s.op("vector", lambda e, tdst=tdst: e.tensor_copy(
                    tdst[:], rtab[:]), "sVE")
            s.wait("sync", "sCast", n=cast_cnt[l])
            s.dma("sync",
                  bass.AP(wsb, 0, [[wsb.shape[1], 128], [128, 96], [1, 128]]),
                  bass.AP(wbf_qkv[l].tensor, 0, [[128, 128], [128 * 128, 96], [1, 128]]),
                  "sWL")

        def ffn_qkv(l, ch, xc, XCW, tA, tB, tAk, tBk, qpl, kpl, vpl, hbf,
                    cpar, KW, VW, QPW, wsb):
            for tt in range(2):
                c0 = tt * 512
                s.wait("vector", "sQD")
                s.wait("vector", "sCC")
                s.wait("vector", "sMM")
                for kc in range(8):
                    off = (1 if kc < 4 else 0) + c0
                    s.op("vector", lambda e, kc=kc, off=off: e.tensor_copy(
                        hbf[:, kc * 512:(kc + 1) * 512],
                        bass.AP(xc[ch], kc * XK + off, [[XCW, 128], [1, 512]])),
                        "sVE" if kc == 7 else None)
                for ocb in range(8):
                    isq = ocb < 4
                    TA, TB_ = (tA, tB) if isq else (tAk, tBk)
                    s.wait("tensor", "sVE")
                    s.wait("tensor", "sSC")
                    s.wait("tensor", "sWL")

                    def fmm(e, ocb=ocb, c0=c0):
                        for kc in range(8):
                            ins = e.matmul(
                                ps_m[:, :],
                                wsb[:, ocb * 1024 + kc * 128: ocb * 1024 + kc * 128 + 128],
                                hbf[:, kc * 512: kc * 512 + 512],
                                start=(kc == 0), stop=(kc == 7))
                        return ins
                    s.op("tensor", fmm, "sMM")
                    s.wait("scalar", "sMM")
                    s.op("scalar", lambda e: e.activation(
                        st_bf[:, 0:512], ps_m[:, :], AF.Square), "sSC")
                    s.wait("tensor", "sSC")
                    s.wait("tensor", "sVE")
                    s.op("tensor", lambda e: e.matmul(
                        ps_t[:, 0:512], c_ind2[:], st_bf[:, 0:512],
                        start=True, stop=True), "sMM")
                    s.wait("scalar", "sMM")
                    s.op("scalar", lambda e: e.activation(
                        st_f[0:2, 0:512], ps_t[0:2, 0:512], AF.Sqrt), "sSC")
                    s.wait("vector", "sSC")
                    s.op("vector", lambda e: e.tensor_scalar_max(
                        st_f[0:2, 512:1024], st_f[0:2, 0:512], 1e-12))
                    s.op("vector", lambda e: e.reciprocal(
                        st_f[0:2, 0:512], st_f[0:2, 512:1024]))
                    s.op("vector", lambda e: e.tensor_copy(
                        st_bf[0:2, 512:1024], st_f[0:2, 0:512]), "sVE")
                    s.wait("tensor", "sVE")
                    s.op("tensor", lambda e: e.matmul(
                        ps_t[:, 0:512], c_bch2[0:2, :], st_bf[0:2, 512:1024],
                        start=True, stop=True), "sMM")
                    s.wait("scalar", "sMM")
                    s.op("scalar", lambda e: e.copy(scr_f[:, 0:512], ps_m[:, :]),
                         "sSC")
                    s.wait("vector", "sSC")
                    pcol = 68 if isq else 69
                    qksc = 8.0 if isq else 1.0
                    s.op("vector", lambda e: e.tensor_mul(
                        scr_f[:, 512:1024], scr_f[:, 0:512], ps_t[:, 0:512]))
                    s.op("vector", lambda e, pcol=pcol, qksc=qksc: e.tensor_scalar(
                        st_bf[:, 0:512], scr_f[:, 512:1024],
                        cpar[:, pcol:pcol + 1], qksc, ALU.mult, ALU.mult), "sVE")
                    s.wait("tensor", "sVE")
                    s.op("tensor", lambda e: e.matmul(
                        ps_m[:, :], c_rotm[:], st_bf[:, 0:512],
                        start=True, stop=True), "sMM")
                    tcol = ch * 1024 + c0
                    s.wait("vector", "sMM")
                    s.op("vector", lambda e, tcol=tcol, TA=TA: e.tensor_mul(
                        scr_f[:, 0:512], st_bf[:, 0:512], TA[:, tcol:tcol + 512]))
                    s.op("vector", lambda e, tcol=tcol, TB_=TB_: e.tensor_mul(
                        scr_f[:, 512:1024], ps_m[:, :], TB_[:, tcol:tcol + 512]))
                    dst = (qpl if isq else None)
                    if isq:
                        s.op("vector", lambda e, ocb=ocb, c0=c0: e.tensor_add(
                            qpl[:, ocb * 1024 + c0: ocb * 1024 + c0 + 512],
                            scr_f[:, 0:512], scr_f[:, 512:1024]), "sVE")
                    else:
                        b = ocb - 4
                        s.op("vector", lambda e, b=b, c0=c0: e.tensor_add(
                            kpl[:, b * 1536 + 512 + c0: b * 1536 + 512 + c0 + 512],
                            scr_f[:, 0:512], scr_f[:, 512:1024]), "sVE")
                for t4 in range(4):
                    t8 = tt * 4 + t4
                    s.wait("tensor", "sVE")
                    s.wait("tensor", "sSC")

                    def fv(e, t4=t4, c0=c0):
                        for kc in range(8):
                            ins = e.matmul(
                                ps_m[:, :],
                                hbf[:, kc * 512 + t4 * 128: kc * 512 + t4 * 128 + 128],
                                bass.AP(wsb, 8 * 1024 + kc * 128,
                                        [[wsb.shape[1], 128], [1024, 4], [1, 128]]),
                                start=(kc == 0), stop=(kc == 7))
                        return ins
                    s.op("tensor", fv, "sMM")
                    s.wait("vector", "sMM")
                    tau = 4 + t8
                    s.op("vector", lambda e, tau=tau: e.tensor_copy(
                        bass.AP(vpl, tau * 520, [[VW, 128], [65, 8], [1, 64]]),
                        bass.AP(ps_m, 0, [[512, 128], [64, 8], [1, 64]])), "sVE")
            s.wait("sync", "sVE")
            s.dma("sync", qkvh[ch * 128:(ch + 1) * 128, 0:4096], qpl[:], "sQD")
            s.dma("sync",
                  bass.AP(qkvh.tensor, ch * 128 * QKVH_W + 4096,
                          [[QKVH_W, 128], [1024, 4], [1, 1024]]),
                  bass.AP(kpl, 512, [[KW, 128], [1536, 4], [1, 1024]]), "sQD")
            s.dma("sync", qkvh[ch * 128:(ch + 1) * 128, 8192:8192 + 4160],
                  bass.AP(vpl, 4 * 520, [[VW, 128], [1, 8 * 520]]), "sQD")
            s.dma("sync",
                  bass.AP(ckv.tensor, ch * 128 * KVH_W,
                          [[KVH_W, 128], [512, 4], [1, 512]]),
                  bass.AP(kpl, 1024, [[KW, 128], [1536, 4], [1, 512]]), "sQD")
            s.dma("sync",
                  bass.AP(ckv.tensor, ch * 128 * KVH_W + 2048,
                          [[KVH_W, 128], [1, 4 * 520]]),
                  bass.AP(vpl, 8 * 520, [[VW, 128], [1, 4 * 520]]), "sQD")

        def ffn_attn(l, ch, xc, XCW, qpl, kpl, vpl, cpar, KW, VW, gkv, wob,
                     ptl, lout, lout_bf):
            s.wait("sync", "sVE")
            s.wait("sync", "sMM")
            s.dma("sync", qpl[:], qkvh[ch * 128:(ch + 1) * 128, 0:4096], "sQD")
            s.dma("sync",
                  bass.AP(kpl, 512, [[KW, 128], [1536, 4], [1, 1024]]),
                  bass.AP(qkvh.tensor, ch * 128 * QKVH_W + 4096,
                          [[QKVH_W, 128], [1024, 4], [1, 1024]]), "sQD")
            s.dma("sync",
                  bass.AP(vpl, 4 * 520, [[VW, 128], [1, 8 * 520]]),
                  qkvh[ch * 128:(ch + 1) * 128, 8192:8192 + 4160], "sQD")
            mk_own = s.mark("sQD")
            s.wait("gpsimd", "sCC")
            s.wait("gpsimd", "sQD")
            s.pdyn("gpsimd", p_sb, 50 + ch, 0, 16 * 128 * KVH_W,
                   lambda sv: bass.AP(kpl, 0, [[KW, 128], [1536, 4], [1, 512]]),
                   lambda sv: bass.AP(gkv.tensor, sv,
                                      [[KVH_W, 128], [512, 4], [1, 512]]), "sGH")
            s.pdyn("gpsimd", p_sb, 48 + ch, 0, 16 * 128 * KVH_W + 2048,
                   lambda sv: bass.AP(vpl, 0, [[VW, 128], [1, 4 * 520]]),
                   lambda sv: bass.AP(gkv.tensor, sv,
                                      [[KVH_W, 128], [1, 4 * 520]]), "sGH")
            mk_halo = s.mark("sGH")

            for w in (1, 0):
                if w == 1:
                    s.wait("tensor", "sQD", mk_own)
                else:
                    s.wait("tensor", "sGH", mk_halo)
                s.wait("vector", "sMM")
                if w == 1:
                    s.wait("vector", "sQD", mk_own)
                else:
                    s.wait("vector", "sGH", mk_halo)
                s.op("vector", lambda e: e.memset(lout[:], 0.0), "sVE")
                for hp in range(4):
                    s.wait("vector", "sMM")
                    s.op("vector", lambda e: e.memset(ps_av0[0:65, :], 0.0))
                    s.op("vector", lambda e: e.memset(ps_av1[0:65, :], 0.0), "sVE")
                    s.wait("tensor", "sVE")
                    for sc in range(8):
                        qlo = max(0, 128 * sc - 512)
                        qhi = min(512, 128 * sc + 128)
                        par = sc % 2
                        s.wait("tensor", "sSC")

                        def fqkt(e, hp=hp, w=w, sc=sc, qlo=qlo, qhi=qhi, par=par):
                            for hh in range(2):
                                ins = e.matmul(
                                    ps_sc[:, par * 1024 + hh * 512 + qlo:
                                          par * 1024 + hh * 512 + qhi],
                                    kpl[64 * hh:64 * hh + 64,
                                        hp * 1536 + w * 512 + sc * 128:
                                        hp * 1536 + w * 512 + sc * 128 + 128],
                                    qpl[64 * hh:64 * hh + 64,
                                        hp * 1024 + w * 512 + qlo:
                                        hp * 1024 + w * 512 + qhi],
                                    start=True, stop=True,
                                    tile_position=(64 * hh, 0))
                            return ins
                        s.op("tensor", fqkt, "sMM")
                        s.wait("vector", "sMM")

                        def fmask(e, w=w, ch=ch, sc=sc, qlo=qlo, qhi=qhi, par=par):
                            mb = sc if sc <= 3 else sc - 4
                            mask = c_mright if sc <= 3 else c_mleft
                            for hh in range(2):
                                base = par * 1024 + hh * 512
                                ins = e.tensor_add(
                                    ps_sc[:, base + 128 * mb: base + 128 * mb + 128],
                                    ps_sc[:, base + 128 * mb: base + 128 * mb + 128],
                                    mask[:])
                                if w == 0 and sc <= 3:
                                    ins = e.tensor_scalar_add(
                                        ps_sc[:, base + qlo: base + qhi],
                                        ps_sc[:, base + qlo: base + qhi],
                                        cpar[:, 70 + ch:71 + ch])
                            return ins
                        s.op("vector", fmask, "sVE")
                        s.wait("scalar", "sVE")
                        s.wait("scalar", "sMM")

                        def fexp(e, qlo=qlo, qhi=qhi, par=par):
                            return e.activation(
                                bass.AP(ptl, qlo, [[1024, 128], [512, 2], [1, qhi - qlo]]),
                                bass.AP(ps_sc, par * 1024 + qlo,
                                        [[2048, 128], [512, 2], [1, qhi - qlo]]),
                                AF.Exp)
                        s.op("scalar", fexp, "sSC")
                        s.wait("tensor", "sSC")

                        def fav(e, hp=hp, w=w, sc=sc, qlo=qlo, qhi=qhi):
                            tau = 4 * w + sc
                            for hh, bank in ((0, ps_av0), (1, ps_av1)):
                                h = 2 * hp + hh
                                ins = e.matmul(
                                    bank[0:65, qlo:qhi],
                                    vpl[:, tau * 520 + 65 * h: tau * 520 + 65 * h + 65],
                                    ptl[:, hh * 512 + qlo: hh * 512 + qhi],
                                    start=False, stop=False,
                                    skip_group_check=True)
                            return ins
                        s.op("tensor", fav, "sMM")
                    s.wait("vector", "sMM")
                    for hh, bank in ((0, ps_av0), (1, ps_av1)):
                        h = 2 * hp + hh
                        s.op("vector", lambda e, h=h, bank=bank: e.tensor_add(
                            bass.AP(lout, h * 512, [[lout.shape[1], 65], [1, 512]]),
                            bass.AP(lout, h * 512, [[lout.shape[1], 65], [1, 512]]),
                            bass.AP(bank, 0, [[512, 65], [1, 512]])), "sVE")
                for h in range(8):
                    s.wait("vector", "sVE")
                    s.op("vector", lambda e, h=h: e.reciprocal(
                        bass.AP(st_f, 64 * st_f.shape[1], [[st_f.shape[1], 1], [1, 512]]),
                        bass.AP(lout, 64 * lout.shape[1] + h * 512,
                                [[lout.shape[1], 1], [1, 512]])))
                    s.op("vector", lambda e: e.tensor_copy(
                        bass.AP(st_bf, 64 * st_bf.shape[1], [[st_bf.shape[1], 1], [1, 512]]),
                        bass.AP(st_f, 64 * st_f.shape[1], [[st_f.shape[1], 1], [1, 512]])),
                        "sVE")
                    s.wait("tensor", "sVE")
                    s.wait("tensor", "sSC")
                    s.op("tensor", lambda e: e.matmul(
                        ps_t[:, 0:512], c_ones[64:65, :], st_bf[64:65, 512:1024]
                        if False else st_bf[64:65, 0:512],
                        start=True, stop=True), "sMM")
                    s.wait("vector", "sMM")
                    s.op("vector", lambda e, h=h: e.tensor_mul(
                        lout_bf[0:64, h * 512:(h + 1) * 512],
                        lout[0:64, h * 512:(h + 1) * 512],
                        ps_t[0:64, 0:512]), "sVE")
                for oc in range(8):
                    s.wait("tensor", "sVE")
                    s.wait("tensor", "sWL")

                    def fc6(e, oc=oc):
                        for b in range(8):
                            ins = e.matmul(
                                ps_m[:, :],
                                wob[:, (b * 8 + oc) * 128: (b * 8 + oc) * 128 + 128],
                                lout_bf[:, b * 512: b * 512 + 512],
                                start=(b == 0), stop=(b == 7))
                        return ins
                    s.op("tensor", fc6, "sMM")
                    s.wait("vector", "sMM")
                    s.op("vector", lambda e, ch=ch, oc=oc, w=w: e.tensor_add(
                        bass.AP(xc[ch], oc * XK + 1 + w * 512, [[XCW, 128], [1, 512]]),
                        bass.AP(xc[ch], oc * XK + 1 + w * 512, [[XCW, 128], [1, 512]]),
                        ps_m[:, :]), "sVE")

        def ffn_mlp(l, ch, xc, XCW, cpar, hbf, hsq, hln, act, w1t, w2t, sv511):
            s.wait("vector", "sMM")
            s.wait("vector", "sSC")
            for i, kc in enumerate(range(4, 8)):
                s.op("vector", lambda e, kc=kc, i=i: e.tensor_copy(
                    sv511[:, i:i + 1],
                    bass.AP(xc[ch], kc * XK + 512, [[XCW, 128], [1, 1]])),
                    "sVE" if kc == 7 else None)
            for tt in range(2):
                c0 = tt * 512
                s.wait("vector", "sQD")
                s.wait("vector", "sMM")
                for kc in range(8):
                    off = (1 if kc < 4 else 0) + c0
                    s.op("vector", lambda e, kc=kc, off=off: e.tensor_copy(
                        hbf[:, kc * 512:(kc + 1) * 512],
                        bass.AP(xc[ch], kc * XK + off, [[XCW, 128], [1, 512]])),
                        "sVE" if kc == 7 else None)
                if tt == 1:
                    for i, kc in enumerate(range(4, 8)):
                        s.op("vector", lambda e, kc=kc, i=i: e.tensor_copy(
                            hbf[:, kc * 512: kc * 512 + 1], sv511[:, i:i + 1]),
                            "sVE" if kc == 7 else None)
                s.wait("scalar", "sVE")
                for kc in range(8):
                    off = (1 if kc < 4 else 0) + c0
                    s.op("scalar", lambda e, kc=kc, off=off, ch=ch: e.activation(
                        hsq[:, kc * 512:(kc + 1) * 512],
                        bass.AP(xc[ch], kc * XK + off, [[XCW, 128], [1, 512]]),
                        AF.Square), "sSC" if kc == 7 else None)
                if tt == 1:
                    for i, kc in enumerate(range(4, 8)):
                        s.op("scalar", lambda e, kc=kc, i=i: e.activation(
                            hsq[:, kc * 512: kc * 512 + 1], sv511[:, i:i + 1],
                            AF.Square), "sSC" if kc == 7 else None)
                s.wait("tensor", "sVE")
                s.wait("tensor", "sSC")

                def fstat2(e):
                    for kc in range(8):
                        e.matmul(ps_t[:, 0:512], c_statx[:],
                                 hbf[:, kc * 512: kc * 512 + 512],
                                 start=(kc == 0), stop=(kc == 7))
                    for kc in range(8):
                        ins = e.matmul(ps_m[:, 0:512], c_statx[:],
                                       hsq[:, kc * 512: kc * 512 + 512],
                                       start=(kc == 0), stop=(kc == 7))
                    return ins
                s.op("tensor", fstat2, "sMM")
                s.wait("vector", "sMM")
                s.op("vector", lambda e: e.tensor_scalar(
                    st_f[0:1, 0:512], ps_t[0:1, 0:512], 1.0 / 1024, None, ALU.mult))
                s.op("vector", lambda e: e.tensor_scalar(
                    st_f[0:1, 512:1024], ps_m[0:1, 0:512], 1.0 / 1024, None, ALU.mult))
                s.op("vector", lambda e: e.tensor_mul(
                    scr2_f[0:1, 0:512], st_f[0:1, 0:512], st_f[0:1, 0:512]))
                s.op("vector", lambda e: e.tensor_sub(
                    scr2_f[0:1, 0:512], st_f[0:1, 512:1024], scr2_f[0:1, 0:512]), "sVE")
                s.wait("scalar", "sVE")
                s.op("scalar", lambda e: e.activation(
                    scr2_f[0:1, 0:512], scr2_f[0:1, 0:512], AF.Sqrt,
                    bias=c_eps[0:1, 1:2]), "sSC")
                s.wait("vector", "sSC")
                s.op("vector", lambda e: e.reciprocal(
                    st_f[0:1, 512:1024], scr2_f[0:1, 0:512]))
                s.op("vector", lambda e: e.tensor_copy(
                    st_bf[0:1, 0:512], st_f[0:1, 512:1024]))
                s.op("vector", lambda e: e.tensor_sub(
                    scr2_f[0:1, 0:512], st_f[0:1, 512:1024], st_bf[0:1, 0:512]))
                s.op("vector", lambda e: e.tensor_copy(
                    st_bf[0:1, 512:1024], scr2_f[0:1, 0:512]))
                s.op("vector", lambda e: e.tensor_scalar(
                    st_f[0:1, 0:512], st_f[0:1, 0:512], -1.0, None, ALU.mult))
                s.op("vector", lambda e: e.tensor_copy(
                    st_bf[1:2, 0:512] if False else scr_f[0:1, 0:512], st_f[0:1, 0:512]))
                s.op("vector", lambda e: e.tensor_copy(
                    hbf[0:1, 0:512], st_f[0:1, 0:512]))
                s.op("vector", lambda e: e.tensor_sub(
                    scr2_f[0:1, 0:512], st_f[0:1, 0:512], hbf[0:1, 0:512]))
                s.op("vector", lambda e: e.tensor_copy(
                    hbf[0:1, 512:1024], scr2_f[0:1, 0:512]), "sVE")
                s.wait("tensor", "sVE")

                def fbc2(e):
                    e.matmul(ps_t[:, 0:512], c_ones[0:1, :], hbf[0:1, 0:512],
                             start=True, stop=False)
                    e.matmul(ps_t[:, 0:512], c_ones[0:1, :], hbf[0:1, 512:1024],
                             start=False, stop=True)
                    e.matmul(ps_m[:, 0:512], c_ones[0:1, :], st_bf[0:1, 0:512],
                             start=True, stop=False)
                    return e.matmul(ps_m[:, 0:512], c_ones[0:1, :], st_bf[0:1, 512:1024],
                                    start=False, stop=True)
                s.op("tensor", fbc2, "sMM")
                s.wait("vector", "sMM")
                for kc in range(8):
                    off = (1 if kc < 4 else 0) + c0
                    s.op("vector", lambda e, kc=kc, off=off, ch=ch: e.tensor_add(
                        scr_f[:, 0:512],
                        bass.AP(xc[ch], kc * XK + off, [[XCW, 128], [1, 512]]),
                        ps_t[:, 0:512]))
                    s.op("vector", lambda e: e.tensor_mul(
                        scr_f[:, 512:1024], scr_f[:, 0:512], ps_m[:, 0:512]))
                    s.op("vector", lambda e, kc=kc: e.tensor_scalar(
                        hln[:, kc * 512:(kc + 1) * 512], scr_f[:, 512:1024],
                        cpar[:, kc:kc + 1], cpar[:, 8 + kc:9 + kc],
                        ALU.mult, ALU.add), "sVE")
                if tt == 1:
                    for i in range(4):
                        s.op("vector", lambda e, i=i: e.tensor_add(
                            scr_f[:, i:i + 1], sv511[:, i:i + 1], ps_t[:, 0:1]))
                    for i in range(4):
                        s.op("vector", lambda e, i=i: e.tensor_mul(
                            scr_f[:, 4 + i:5 + i], scr_f[:, i:i + 1], ps_m[:, 0:1]))
                    for i, kc in enumerate(range(4, 8)):
                        s.op("vector", lambda e, kc=kc, i=i: e.tensor_scalar(
                            hln[:, kc * 512: kc * 512 + 1], scr_f[:, 4 + i:5 + i],
                            cpar[:, kc:kc + 1], cpar[:, 8 + kc:9 + kc],
                            ALU.mult, ALU.add), "sVE" if kc == 7 else None)
                c8cnt = {}
                for j in range(22):
                    pa_, pg_ = (ps_m, ps_t) if j % 2 == 0 else (ps_av0, ps_av1)
                    wslot = (j % 2) * 2048
                    if j >= 2:
                        s.wait("sync", "sMM", n=c8cnt[j - 2])
                    else:
                        s.wait("sync", "sMM")
                    s.dma("sync",
                          bass.AP(w1t, wslot, [[w1t.shape[1], 128], [128, 8], [1, 128]]),
                          bass.AP(wbf_w1a[l].tensor, j * 8 * 128 * 128,
                                  [[128, 128], [128 * 128, 8], [1, 128]]), "sWL")
                    s.dma("sync",
                          bass.AP(w1t, wslot + 1024, [[w1t.shape[1], 128], [128, 8], [1, 128]]),
                          bass.AP(wbf_w1g[l].tensor, j * 8 * 128 * 128,
                                  [[128, 128], [128 * 128, 8], [1, 128]]), "sWL")
                    s.wait("tensor", "sWL")
                    s.wait("tensor", "sVE")
                    s.wait("tensor", "sSC")

                    def fc8(e, j=j, wslot=wslot, pa_=pa_, pg_=pg_):
                        for kc in range(8):
                            e.matmul(pa_[:, 0:512],
                                     w1t[:, wslot + kc * 128: wslot + kc * 128 + 128],
                                     hln[:, kc * 512: kc * 512 + 512],
                                     start=(kc == 0), stop=(kc == 7))
                        for kc in range(8):
                            ins = e.matmul(pg_[:, 0:512],
                                           w1t[:, wslot + 1024 + kc * 128: wslot + 1024 + kc * 128 + 128],
                                           hln[:, kc * 512: kc * 512 + 512],
                                           start=(kc == 0), stop=(kc == 7))
                        return ins
                    s.op("tensor", fc8, "sMM")
                    c8cnt[j] = s.cnt["sMM"]
                    s.wait("scalar", "sMM")
                    s.op("scalar", lambda e, j=j, pg_=pg_: e.activation(
                        st_bf[:, 0:512], pg_[:, 0:512], AF.Gelu,
                        bias=cpar[:, 46 + j:47 + j]), "sSC")
                    s.wait("vector", "sSC")
                    s.op("vector", lambda e, j=j, pa_=pa_: e.scalar_tensor_tensor(
                        act[:, j * 512:(j + 1) * 512], pa_[:, 0:512],
                        cpar[:, 24 + j:25 + j], st_bf[:, 0:512],
                        ALU.add, ALU.mult), "sVE")
                c9cnt = {}
                for oc in range(8):
                    wslot = (oc % 2) * 2816
                    if oc >= 2:
                        s.wait("sync", "sMM", n=c9cnt[oc - 2])
                    else:
                        s.wait("sync", "sMM")
                    s.dma("sync",
                          bass.AP(w2t, wslot, [[w2t.shape[1], 128], [128, 22], [1, 128]]),
                          bass.AP(wbf_w2[l].tensor, oc * 22 * 128 * 128,
                                  [[128, 128], [128 * 128, 22], [1, 128]]), "sWL")
                    s.wait("tensor", "sWL")
                    s.wait("tensor", "sVE")

                    def fc9(e, oc=oc, wslot=wslot):
                        for kc in range(22):
                            ins = e.matmul(
                                ps_m[:, :],
                                w2t[:, wslot + kc * 128: wslot + kc * 128 + 128],
                                act[:, kc * 512: kc * 512 + 512],
                                start=(kc == 0), stop=(kc == 21))
                        return ins
                    s.op("tensor", fc9, "sMM")
                    c9cnt[oc] = s.cnt["sMM"]
                    s.wait("vector", "sMM")
                    s.op("vector", lambda e, ch=ch, oc=oc, c0=c0: e.scalar_tensor_tensor(
                        bass.AP(xc[ch], oc * XK + 1 + c0, [[XCW, 128], [1, 512]]),
                        ps_m[:, :], cpar[:, 16 + oc:17 + oc],
                        bass.AP(xc[ch], oc * XK + 1 + c0, [[XCW, 128], [1, 512]]),
                        ALU.add, ALU.add), "sVE")

        def emit_row_exchange(xc, XCW, grow_t):
            for ch in range(2):
                s.wait("vector", "sVE")
                s.op("vector", lambda e, ch=ch: e.tensor_copy(
                    st_f[:, 16 + 8 * ch:24 + 8 * ch],
                    bass.AP(xc[ch], MB, [[XCW, 128], [XK, 8], [1, 1]])), "sVE")
            s.wait("sync", "sVE")
            s.wait("sync", "sCC")
            for ch in range(2):
                s.dma("sync", crow[ch * 128:(ch + 1) * 128, :],
                      st_f[:, 16 + 8 * ch:24 + 8 * ch], "sQD")
            allgather(crow, grow_t[0:2048, :])
            s.wait("sync", "sCC")
            s.wait("gpsimd", "sCC")
            s.wait("gpsimd", "sQD")
            for ch in range(2):
                s.pdyn("gpsimd", p_sb, 52 + ch, 0, 16 * 128 * 8,
                       lambda sv, ch=ch: st_f[:, 32 + 8 * ch:40 + 8 * ch],
                       lambda sv: bass.AP(grow_t.tensor, sv, [[8, 128], [1, 8]]),
                       "sGH")
            s.wait("vector", "sGH")
            for ch in range(2):
                s.op("vector", lambda e, ch=ch: e.tensor_copy(
                    bass.AP(xc[ch], 0, [[XCW, 128], [XK, 8], [1, 1]]),
                    st_f[:, 32 + 8 * ch:40 + 8 * ch]), "sVE")

        def emit_ffn_layer(l, xc, XCW, tA, tB, tAk, tBk, qpl, kpl, vpl,
                           hbf, hsq, cpar, KW, VW, QPW, ffn_guards):
            import os
            fsub = int(os.environ.get("K_FFN", "9"))
            if l + 1 < int(os.environ.get("K_NLAYER", "2")):
                fsub = 9
            gkv = gkv0 if l == 0 else gkv1
            wsb = _salloc(f"wsb{l}", [128, 12 * 1024], BF16)
            rtab = _salloc(f"rtab{l}", [128, 2048], F32)
            ffn_prep(l, tA, tB, tAk, tBk, cpar, wsb, None, rtab)
            emit_row_exchange(xc, XCW, grows[2 * l])
            s.wait("vector", "sQD")
            ffn_qkv(l, 0, xc, XCW, tA, tB, tAk, tBk, qpl, kpl, vpl, hbf,
                    cpar, KW, VW, QPW, wsb)
            ffn_qkv(l, 1, xc, XCW, tA, tB, tAk, tBk, qpl, kpl, vpl, hbf,
                    cpar, KW, VW, QPW, wsb)
            allgather(ckv, gkv[0:2048, :])
            barrier()
            while _phase_guards[len(ffn_guards):]:
                _phase_guards.pop().__exit__(None, None, None)
            if fsub < 2:
                return
            wob = _salloc(f"wob{l}", [128, 8 * 1024], BF16)
            ptl = _salloc(f"ptl{l}", [128, 1024], BF16)
            lout = _salloc(f"lout{l}", [128, 8 * 512], F32)
            lout_bf = _salloc(f"lbf{l}", [128, 8 * 512], BF16)
            s.op("vector", lambda e, lout_bf=lout_bf: e.memset(lout_bf[:], 0.0), "sVE")
            s.wait("sync", "sCast", n=cast_cnt[l])
            s.dma("sync",
                  bass.AP(wob, 0, [[wob.shape[1], 128], [128, 64], [1, 128]]),
                  bass.AP(wbf_wo[l].tensor, 0, [[128, 128], [128 * 128, 64], [1, 128]]),
                  "sWL")
            ffn_attn(l, 0, xc, XCW, qpl, kpl, vpl, cpar, KW, VW, gkv, wob,
                     ptl, lout, lout_bf)
            ffn_attn(l, 1, xc, XCW, qpl, kpl, vpl, cpar, KW, VW, gkv, wob,
                     ptl, lout, lout_bf)
            emit_row_exchange(xc, XCW, grows[2 * l + 1])
            barrier()
            if fsub < 3:
                while _phase_guards[len(ffn_guards):]:
                    _phase_guards.pop().__exit__(None, None, None)
                return
            while _phase_guards[len(ffn_guards):]:
                _phase_guards.pop().__exit__(None, None, None)
            hln = _salloc(f"hln{l}", [128, 4096], BF16)
            act = _salloc(f"act{l}", [128, 22 * 512], BF16)
            w1t = _salloc(f"w1t{l}", [128, 4096], BF16)
            w2t = _salloc(f"w2t{l}", [128, 2 * 2816], BF16)
            sv511 = _salloc(f"sv{l}", [128, 4], F32)
            ffn_mlp(l, 0, xc, XCW, cpar, hbf, hsq, hln, act, w1t, w2t, sv511)
            ffn_mlp(l, 1, xc, XCW, cpar, hbf, hsq, hln, act, w1t, w2t, sv511)
            barrier()
            while _phase_guards[len(ffn_guards):]:
                _phase_guards.pop().__exit__(None, None, None)

        def emit_stage_d(xc, XCW, hbf, hsq):
            xsp = _salloc("xsp", [128, 4096], F32)
            otm = _salloc("otm", [128, 1024], F32)
            st2 = _salloc("st2", [128, 128], F32)
            sts = _salloc("sts", [128, 2], F32)
            dsnap = _salloc("dsnap", [128, 2050], F32) if DEBUG_STAGE == 5 else None
            s.op("vector", lambda e: e.memset(st2[:], 0.0), "sVE")
            for ch in range(2):
                for tt in range(2):
                    c0 = tt * 512
                    s.wait("sync", "sVE")
                    s.wait("sync", "sMM")
                    for oc in range(8):
                        s.dma("sync", xsp[:, oc * 512:(oc + 1) * 512],
                              x1_spill[ch * D + oc * 128: ch * D + (oc + 1) * 128,
                                       c0:c0 + 512], "sQD")
                    s.wait("vector", "sQD")
                    for oc in range(8):
                        s.op("vector", lambda e, ch=ch, oc=oc, c0=c0: e.tensor_add(
                            xsp[:, oc * 512:(oc + 1) * 512],
                            xsp[:, oc * 512:(oc + 1) * 512],
                            bass.AP(xc[ch], oc * XK + 1 + c0, [[XCW, 128], [1, 512]])),
                            "sVE" if oc == 7 else None)
                    s.wait("vector", "sVE")
                    s.wait("scalar", "sVE")
                    for oc in range(8):
                        s.op("vector", lambda e, oc=oc: e.tensor_copy(
                            hbf[:, oc * 512:(oc + 1) * 512],
                            xsp[:, oc * 512:(oc + 1) * 512]),
                            "sVE" if oc == 7 else None)
                        s.op("scalar", lambda e, oc=oc: e.activation(
                            hsq[:, oc * 512:(oc + 1) * 512],
                            xsp[:, oc * 512:(oc + 1) * 512], AF.Square),
                            "sSC" if oc == 7 else None)
                    s.wait("tensor", "sVE")
                    s.wait("tensor", "sSC")

                    def fstat3(e):
                        for oc in range(8):
                            e.matmul(ps_t[:, 0:512], c_statx[:],
                                     hbf[:, oc * 512: oc * 512 + 512],
                                     start=(oc == 0), stop=(oc == 7))
                        for oc in range(8):
                            ins = e.matmul(ps_m[:, 0:512], c_statx[:],
                                           hsq[:, oc * 512: oc * 512 + 512],
                                           start=(oc == 0), stop=(oc == 7))
                        return ins
                    s.op("tensor", fstat3, "sMM")
                    s.wait("vector", "sMM")
                    s.op("vector", lambda e: e.tensor_scalar(
                        st_f[0:1, 0:512], ps_t[0:1, 0:512], 1.0 / 1024, None, ALU.mult))
                    s.op("vector", lambda e: e.tensor_scalar(
                        st_f[0:1, 512:1024], ps_m[0:1, 0:512], 1.0 / 1024, None, ALU.mult))
                    s.op("vector", lambda e: e.tensor_mul(
                        scr2_f[0:1, 0:512], st_f[0:1, 0:512], st_f[0:1, 0:512]))
                    s.op("vector", lambda e: e.tensor_sub(
                        scr2_f[0:1, 0:512], st_f[0:1, 512:1024], scr2_f[0:1, 0:512]),
                        "sVE")
                    s.wait("scalar", "sVE")
                    s.op("scalar", lambda e: e.activation(
                        scr2_f[0:1, 0:512], scr2_f[0:1, 0:512], AF.Sqrt,
                        bias=c_eps[0:1, 1:2]), "sSC")
                    s.wait("vector", "sSC")
                    s.op("vector", lambda e: e.reciprocal(
                        st_f[0:1, 512:1024], scr2_f[0:1, 0:512]))
                    s.op("vector", lambda e: e.tensor_scalar(
                        st_f[0:1, 0:512], st_f[0:1, 0:512], -1.0, None, ALU.mult),
                        "sVE")
                    for t4 in range(4):
                        s.wait("vector", "sVE")
                        s.wait("vector", "sQD")
                        s.wait("vector", "sST")
                        s.op("vector", lambda e, t4=t4: e.tensor_copy(
                            st2[0:1, 0:128], st_f[0:1, t4 * 128:(t4 + 1) * 128]),
                            "sVE")
                        s.wait("tensor", "sVE")
                        s.wait("tensor", "sSC")
                        s.op("tensor", lambda e: e.transpose(
                            ps_t[:, 0:128], st2[:, 0:128], c_ident_f[:]), "sMM")
                        s.wait("vector", "sMM")
                        s.op("vector", lambda e: e.tensor_copy(
                            sts[:, 0:1], ps_t[:, 0:1]), "sVE")
                        s.op("vector", lambda e, t4=t4: e.tensor_copy(
                            st2[0:1, 0:128],
                            st_f[0:1, 512 + t4 * 128: 512 + (t4 + 1) * 128]), "sVE")
                        s.wait("tensor", "sVE")
                        s.op("tensor", lambda e: e.transpose(
                            ps_t[:, 0:128], st2[:, 0:128], c_ident_f[:]), "sMM")
                        s.wait("vector", "sMM")
                        s.op("vector", lambda e: e.tensor_copy(
                            sts[:, 1:2], ps_t[:, 0:1]), "sVE")
                        for oc in range(8):
                            s.wait("tensor", "sVE")
                            s.op("tensor", lambda e, oc=oc, t4=t4: e.transpose(
                                ps_m[:, 0:128],
                                xsp[:, oc * 512 + t4 * 128: oc * 512 + t4 * 128 + 128],
                                c_ident_f[:]), "sMM")
                            s.wait("vector", "sMM")
                            s.op("vector", lambda e, oc=oc: e.tensor_copy(
                                otm[:, oc * 128:(oc + 1) * 128], ps_m[:, 0:128]),
                                "sVE")
                        s.op("vector", lambda e: e.tensor_scalar(
                            otm[:], otm[:], sts[:, 0:1], None, ALU.add))
                        s.op("vector", lambda e: e.tensor_scalar(
                            otm[:], otm[:], sts[:, 1:2], None, ALU.mult), "sVE")
                        s.wait("sync", "sVE")
                        s.dma("sync",
                              out_t[ch * MB + c0 + t4 * 128: ch * MB + c0 + (t4 + 1) * 128, :],
                              otm[:], "sST")
                        if DEBUG_STAGE == 4 and ch == 0 and tt == 0 and t4 == 0:
                            r0 = t4 * 1024
                            s.dma("sync", outs["dbg_x1"][r0:r0 + 128, 0:2],
                                  sts[:, 0:2], "sST")
                            s.dma("sync", outs["dbg_x1"][r0 + 128:r0 + 129, :],
                                  st_f[0:1, 0:1024], "sST")
                            s.dma("sync", outs["dbg_x1"][r0 + 256:r0 + 384, :],
                                  otm[:], "sST")
                            for oc in range(2):
                                s.dma("sync",
                                      outs["dbg_x1"][r0 + 512 + oc * 128:
                                                     r0 + 512 + (oc + 1) * 128, 0:512],
                                      xsp[:, oc * 512:(oc + 1) * 512], "sST")
            if DEBUG_STAGE == 5:
                s.wait("sync", "sVE")
                s.dma("sync", outs["dbg_x1"][256:384, :], dsnap[:, 0:1024], "sST")
                s.dma("sync", outs["dbg_x1"][384:512, :], dsnap[:, 1026:2050], "sST")
                s.dma("sync", outs["dbg_x1"][0:128, 0:2], dsnap[:, 1024:1026], "sST")

        import os
        trunc = int(os.environ.get("K_TRUNC", "99"))
        phases = [lambda: build_q(0), lambda: build_q(1)]
        phases += [(lambda j=j: mla_iter(j)) for j in range(9)]
        phases += [lambda: mla_finish(0), lambda: mla_finish(1)]
        phases += [lambda: emit_ffn()]
        for i, ph in enumerate(phases):
            if i < trunc:
                ph()

        for sn in ("sST", "sQD", "sOD", "sTr", "sCast", "sWL", "sMM", "sSC",
                   "sVE", "sCC", "sGH"):
            s.wait("sync", sn)

        _free_phase()
        for name, attach in (("sync", block.sync), ("tensor", block.tensor),
                             ("vector", block.vector), ("scalar", block.scalar),
                             ("gpsimd", block.gpsimd)):
            def body(e, name=name):
                for f in s.q[name]:
                    try:
                        f(e)
                    except Exception:
                        print("EMIT FAIL", name, getattr(f, "_src", "?"))
                        raise
            attach(body)

    return nc


_NC_CACHE = None


def kernel(**inputs):
    global _NC_CACHE
    sh = shared_inputs(inputs)
    in_maps = []
    metas = []
    for core in range(NCORE):
        ci, meta = core_inputs(core)
        m = dict(sh)
        m.update(ci)
        in_maps.append(m)
        metas.append(meta)
    if _NC_CACHE is None:
        _NC_CACHE = build_nc()
    nc = _NC_CACHE
    res = run_bass_kernel_spmd(nc, in_maps, core_ids=list(range(NCORE)))
    if DEBUG_STAGE in (1, 2):
        return np.stack([res.results[c]["dbg_x1"] for c in range(NCORE)])
    if DEBUG_STAGE in (4, 5):
        return (np.stack([res.results[c]["dbg_x1"] for c in range(NCORE)]),
                np.stack([res.results[c]["out_t"] for c in range(NCORE)]))
    out = np.zeros((B, T, D), np.float32)
    for core in range(NCORE):
        pa, pb = core + 1, NMB - core
        o = res.results[core]["out_t"]
        out[0, (pa - 1) * MB:pa * MB] = o[0:MB]
        out[1, (pb - 1) * MB:pb * MB] = o[MB:2 * MB]
    return out
